# version 01 (final)
# baseline (speedup 1.0000x reference)
import numpy as np
import concourse.bass as bass
import concourse.mybir as mybir
from concourse.bass_utils import run_bass_kernel_spmd

F32 = mybir.dt.float32
BF16 = mybir.dt.bfloat16
AF = mybir.ActivationFunctionType
ALU = mybir.AluOpType
AX = mybir.AxisListType

NCORES = 8
D = 1024
BATCH, T = 16, 4096
TSLICE = T // NCORES
TPC = BATCH * TSLICE
LN_EPS = 1e-5



PARANOID = 0
DEBUG_S4 = 0
SERIAL_SECT = 0
TILE_BARRIER = 0

class Prog:

    ENG = {'pe': 'tensor', 'act': 'scalar', 'dve': 'vector',
           'pool': 'gpsimd', 'sp': 'sync'}

    def __init__(self, nc):
        self.nc = nc
        self.ops = []
        self.sems = {e: nc.alloc_semaphore(f"s_{e}") for e in self.ENG}
        self.cnt = {e: 0 for e in self.ENG}
        self.dsems = {}
        self.last = None

    def serial(self, on):
        self._serial = on

    def op(self, eng, fn, deps=(), tick=True, inc_pair=None):
        if PARANOID or getattr(self, '_serial', False):
            tick = True
            deps = tuple(deps) + ((self.last,) if self.last is not None else ())
        tok = None
        if inc_pair is not None:
            sem, n, cum = inc_pair
            tok = (sem, cum)
            inc = (sem, n)
        elif tick:
            self.cnt[eng] += 1
            tok = (self.sems[eng], self.cnt[eng])
            inc = (self.sems[eng], 1)
        else:
            inc = None
        self.ops.append((eng, fn, tuple(d for d in deps if d is not None), inc))
        if (PARANOID or getattr(self, '_serial', False)) and tok is not None:
            self.last = tok
        return tok

    def dma(self, queue, out, in_, deps=(), key='misc', **kw):
        if key not in self.dsems:
            self.dsems[key] = [self.nc.alloc_semaphore(f"d{len(self.dsems)}"), 0]
        ent = self.dsems[key]
        ent[1] += 16
        return self.op(queue, lambda g: g.dma_start(out=out, in_=in_, **kw),
                       deps, inc_pair=(ent[0], 16, ent[1]))

    def final_dma_waits(self, eng='sp'):
        self.wait(eng, tuple((s, c) for (s, c) in self.dsems.values()))

    def barrier(self):
        toks = [(self.sems[e], self.cnt[e]) for e in self.ENG if self.cnt[e] > 0]
        for e in self.ENG:
            self.wait(e, toks)

    def wait(self, eng, deps):
        self.ops.append((eng, None, tuple(d for d in deps if d is not None),
                         None))

    def emit(self):
        nc = self.nc
        with nc.Block() as block:
            for ename, attr in self.ENG.items():
                ops = [o for o in self.ops if o[0] == ename]
                if not ops:
                    continue

                def body(eng, ops=ops):
                    waited = {}
                    for _, fn, deps, inc in ops:
                        for (sem, val) in deps:
                            if val <= waited.get(sem.num, 0):
                                continue
                            eng.wait_ge(sem, val)
                            waited[sem.num] = val
                        if fn is None:
                            continue
                        inst = fn(eng)
                        if inc is not None and inst is not None:
                            inst.then_inc(inc[0], inc[1])
                getattr(block, attr)(body)


class Slots:

    def __init__(self, n):
        self.n = n
        self.deps = [[] for _ in range(n)]
        self.i = 0

    def acquire(self):
        j = self.i % self.n
        self.i += 1
        deps = tuple(self.deps[j])
        self.deps[j] = []
        return j, deps

    def consumed(self, j, tok):
        if tok is not None:
            self.deps[j].append(tok)



def prep_slabs(w):
    i, o = w.shape
    r = w.reshape(i // 128, 128, o // 128, 128).transpose(2, 1, 0, 3)
    return np.ascontiguousarray(r.reshape(o // 128, 128, i))



def emit_layernorm(p, xd, cols, nt, bufs, psfree, war, router=None):
    xbuf, xbuf_sl = bufs['xbuf'], bufs['xbuf_sl']
    xsq, xsq_sl = bufs['xsq'], bufs['xsq_sl']
    xnb = bufs['xnb']
    ps = bufs['psum']
    ones_f, ones1 = bufs['ones_f'], bufs['ones1']
    mrow, vrow, srow = bufs['rows']
    arow, brow = srow, vrow
    splits = [(s0, min(s0 + 512, nt)) for s0 in range(0, nt, 512)]

    tok_stats = None
    for k in range(8):
        j, warx = xbuf_sl.acquire()
        tx = p.dma('sp', xbuf[:, j * nt:(j + 1) * nt],
                   xd[k * 128:(k + 1) * 128, cols], deps=warx, key=('xbuf', j))
        xv = xbuf[:, j * nt:(j + 1) * nt]
        jq, warq = xsq_sl.acquire()
        tq = p.op('dve', lambda v, xv=xv, jq=jq: v.tensor_mul(
            xsq[:, jq * nt:(jq + 1) * nt], xv, xv), deps=(tx,) + warq)
        qv = xsq[:, jq * nt:(jq + 1) * nt]
        tk = None
        for h, (s, e) in enumerate(splits):
            p.op('pe', lambda t, xv=xv, s=s, e=e, k=k: t.matmul(
                ps[0][0:1, s:e], ones_f[:, 0:1], xv[:, s:e],
                start=(k == 0), stop=(k == 7)),
                deps=(tx, psfree.get(0)) if k == 0 else (tx,), tick=False)
            tk = p.op('pe', lambda t, qv=qv, s=s, e=e, k=k: t.matmul(
                ps[1][0:1, s:e], ones_f[:, 0:1], qv[:, s:e],
                start=(k == 0), stop=(k == 7)),
                deps=(tq, psfree.get(1)) if k == 0 else (tq,),
                tick=(h == len(splits) - 1))
        xbuf_sl.consumed(j, tk)
        xsq_sl.consumed(jq, tk)
        if k == 7:
            tok_stats = tk

    nv = slice(0, nt)
    t1 = p.op('dve', lambda v: v.tensor_scalar_mul(mrow[0:1, nv], ps[0][0:1, nv], 1.0 / D),
              deps=(tok_stats,))
    t2 = p.op('dve', lambda v: v.tensor_scalar_mul(vrow[0:1, nv], ps[1][0:1, nv], 1.0 / D),
              deps=(t1,))
    psfree[0], psfree[1] = t1, t2
    t3 = p.op('dve', lambda v: v.scalar_tensor_tensor(
        srow[0:1, nv], mrow[0:1, nv], -1.0, mrow[0:1, nv], ALU.mult, ALU.mult),
        deps=(t2,))
    t4 = p.op('dve', lambda v: v.tensor_add(vrow[0:1, nv], vrow[0:1, nv], srow[0:1, nv]),
              deps=(t3,))
    t5 = p.op('dve', lambda v: v.tensor_scalar_add(vrow[0:1, nv], vrow[0:1, nv], LN_EPS),
              deps=(t4,))
    t6 = p.op('act', lambda s: s.activation(srow[0:1, nv], vrow[0:1, nv], AF.Sqrt),
              deps=(t5,))
    t7 = p.op('dve', lambda v: v.reciprocal(arow[0:1, nv], srow[0:1, nv]), deps=(t6,))
    t8 = p.op('dve', lambda v: v.scalar_tensor_tensor(
        brow[0:1, nv], mrow[0:1, nv], -1.0, arow[0:1, nv], ALU.mult, ALU.mult),
        deps=(t7,))

    tbc = None
    for h, (s, e) in enumerate(splits):
        p.op('pe', lambda t, s=s, e=e: t.matmul(
            ps[2][:, s:e], ones1[0:1, :], arow[0:1, s:e], start=True, stop=True),
            deps=(t7, psfree.get(2)) if h == 0 else (t7,), tick=False)
        tbc = p.op('pe', lambda t, s=s, e=e: t.matmul(
            ps[3][:, s:e], ones1[0:1, :], brow[0:1, s:e], start=True, stop=True),
            deps=(t8, psfree.get(3)) if h == 0 else (t8,), tick=(h == len(splits) - 1))

    tok_xn = None
    ta = None
    for k in range(8):
        j, warx = xbuf_sl.acquire()
        tx = p.dma('sp', xbuf[:, j * nt:(j + 1) * nt],
                   xd[k * 128:(k + 1) * 128, cols], deps=warx, key=('xbuf', j))
        xv = xbuf[:, j * nt:(j + 1) * nt]
        jq, warq = xsq_sl.acquire()
        ta = p.op('dve', lambda v, xv=xv, jq=jq: v.tensor_mul(
            xsq[:, jq * nt:(jq + 1) * nt], xv, ps[2][:, nv]),
            deps=(tx, tbc) + warq + ((war.get('xnb'),) if k == 0 else ()))
        if router is not None:
            qv2 = xsq[:, jq * nt:(jq + 1) * nt]
            tf = p.op('dve', lambda v, qv2=qv2: v.tensor_add(qv2, qv2, ps[3][:, nv]),
                      deps=(ta,))
            tr = router(k, qv2, tf)
            tok_xn = p.op('dve', lambda v, qv2=qv2, k=k: v.tensor_copy(
                xnb[:, k * nt:(k + 1) * nt], qv2), deps=(tf,))
            xsq_sl.consumed(jq, tr)
        else:
            tok_xn = p.op('dve', lambda v, jq=jq, k=k: v.tensor_add(
                xnb[:, k * nt:(k + 1) * nt], xsq[:, jq * nt:(jq + 1) * nt], ps[3][:, nv]),
                deps=(ta,))
        xbuf_sl.consumed(j, ta)
        xsq_sl.consumed(jq, tok_xn)
    psfree[2], psfree[3] = ta, tok_xn
    return tok_xn


def emit_fm_linear(p, *, rhs, rhs_tok, kchunks, mchunks, nt, slab_dram, slab_sb,
                   slabw, slab_sl, ps_ids, bufs, psfree, epilogue, extra_mm_dep=None):
    ps = bufs['psum']
    last_mm = None
    for m in range(mchunks):
        par = m % 2
        pid = ps_ids[par]
        j, war = slab_sl.acquire()
        td = p.dma('pool', slab_sb[:, j * slabw:j * slabw + kchunks * 128],
                   slab_dram[m], deps=war, key=('slab', j))
        sv = slab_sb[:, j * slabw:j * slabw + kchunks * 128]
        bdep = psfree.get(pid)
        tk = None
        for k in range(kchunks):
            for h in range(nt // 512):
                s, e = h * 512, (h + 1) * 512
                deps = ()
                if k == 0 and h == 0:
                    deps = (td, rhs_tok, bdep, extra_mm_dep)
                tk = p.op('pe', lambda t, sv=sv, k=k, s=s, e=e, pid=pid: t.matmul(
                    ps[pid][:, s:e], sv[:, k * 128:(k + 1) * 128], rhs(k, s, e),
                    start=(k == 0), stop=(k == kchunks - 1)),
                    deps=deps,
                    tick=(k == kchunks - 1 and h == nt // 512 - 1))
        slab_sl.consumed(j, tk)
        last_mm = tk
        free_tok = epilogue(m, ps[pid], tk)
        psfree[pid] = free_tok
    return last_mm


def alloc_common(nc, nt, n_smalls, with_psum=True):
    sb = nc.alloc_sbuf_tensor
    bufs = dict(
        xbuf=sb("xbuf", [128, 2 * nt], F32), xbuf_sl=Slots(2),
        xsq=sb("xsq", [128, 2 * nt], F32), xsq_sl=Slots(2),
        xnb=sb("xnb", [128, 8 * nt], BF16),
        smb=sb("smb", [128, n_smalls], F32),
        ones_f=sb("ones_f", [128, 1], F32),
        ones1=sb("ones1", [1, 128], F32),
        onesb=sb("onesb", [1, 128], BF16),
        rows=[sb(f"row{i}", [1, nt], F32) for i in range(3)],
        psum=[nc.alloc_psum_tensor(f"ps{i}", [128, nt], F32) for i in range(4)]
             if with_psum else None,
    )
    return bufs



def build_l0ffn(ntiles=8, nt=1024):
    nc = bass.Bass("TRN2", target_bir_lowering=False)
    ntok = ntiles * nt
    xd = nc.dram_tensor("x", [D, ntok], F32, kind="ExternalInput")
    w1s = nc.dram_tensor("w1s", [32, 128, 1024], F32, kind="ExternalInput")
    w2s = nc.dram_tensor("w2s", [32, 128, 1024], F32, kind="ExternalInput")
    h1s = nc.dram_tensor("h1s", [32, 128, 4096], F32, kind="ExternalInput")
    h2s = nc.dram_tensor("h2s", [32, 128, 4096], F32, kind="ExternalInput")
    w3s = nc.dram_tensor("w3s", [8, 128, 4096], F32, kind="ExternalInput")
    sm = nc.dram_tensor("smalls", [128, 136], F32, kind="ExternalInput")
    yd = nc.dram_tensor("y", [D, ntok], F32, kind="ExternalOutput")

    sb = nc.alloc_sbuf_tensor
    bufs = alloc_common(nc, nt, 136)
    hb = sb("hb", [128, 32 * nt], BF16)
    gb = sb("gb", [128, 32 * nt], BF16)
    slab = sb("slab", [128, 2 * 4096], BF16)
    t1b = sb("t1b", [128, nt], F32)
    t2b = sb("t2b", [128, nt], F32)
    outb = sb("outb", [128, nt], F32)
    smb, xbuf = bufs['smb'], bufs['xbuf']
    ps = bufs['psum']
    S1C, S2C, H1C, H2C, W3C = 0, 32, 64, 96, 128

    p = Prog(nc)
    t_sm = p.dma('sp', smb[:, :], sm[:, :], key='smb')
    t_of = p.op('pool', lambda g: g.memset(bufs['ones_f'][:, :], 1.0))
    t_o1 = p.op('pool', lambda g: g.memset(bufs['ones1'][:, :], 1.0))
    p.wait('pe', (t_of, t_o1))
    p.wait('act', (t_sm,))

    slab_sl = Slots(2)
    t1_sl = Slots(1)
    out_sl = Slots(1)
    psfree = {}
    war = {}

    for tile in range(ntiles):
        cols = slice(tile * nt, (tile + 1) * nt)
        tok_xn = emit_layernorm(p, xd, cols, nt, bufs, psfree, war)

        h_last = None
        w1w2_last_mm = None
        for m in range(32):
            par = m % 2
            j, warj = slab_sl.acquire()
            td1 = p.dma('pool', slab[:, j * 4096:j * 4096 + 1024], w1s[m],
                        deps=warj, key=('w1sl', j))
            td2 = p.dma('pool', slab[:, j * 4096 + 1024:j * 4096 + 2048], w2s[m],
                        deps=warj, key=('w2sl', j))
            pA, pB = 2 * par, 2 * par + 1
            bdepA, bdepB = psfree.get(pA), psfree.get(pB)
            tkA = tkB = None
            for k in range(8):
                for h in range(nt // 512):
                    s, e = h * 512, (h + 1) * 512
                    tkA = p.op('pe', lambda t, j=j, k=k, s=s, e=e, pA=pA: t.matmul(
                        ps[pA][:, s:e],
                        slab[:, j * 4096 + k * 128:j * 4096 + (k + 1) * 128],
                        bufs['xnb'][:, k * nt + s:k * nt + e],
                        start=(k == 0), stop=(k == 7)),
                        deps=(td1, tok_xn, bdepA) if (k == 0 and h == 0) else (),
                        tick=(k == 7 and h == nt // 512 - 1))
            for k in range(8):
                for h in range(nt // 512):
                    s, e = h * 512, (h + 1) * 512
                    tkB = p.op('pe', lambda t, j=j, k=k, s=s, e=e, pB=pB: t.matmul(
                        ps[pB][:, s:e],
                        slab[:, j * 4096 + 1024 + k * 128:j * 4096 + 1024 + (k + 1) * 128],
                        bufs['xnb'][:, k * nt + s:k * nt + e],
                        start=(k == 0), stop=(k == 7)),
                        deps=(td2, bdepB) if (k == 0 and h == 0) else (),
                        tick=(k == 7 and h == nt // 512 - 1))
            slab_sl.consumed(j, tkB)
            w1w2_last_mm = tkB
            jt, wart = t1_sl.acquire()
            ta1 = p.op('act', lambda s_, m=m, pA=pA: s_.activation(
                t1b[:, :], ps[pA][:, :], AF.Silu, scale=smb[:, S1C + m:S1C + m + 1]),
                deps=(tkA,) + wart + ((war.get('hb'),) if m == 0 else ()))
            ta2 = p.op('act', lambda s_, m=m, pB=pB: s_.activation(
                t2b[:, :], ps[pB][:, :], AF.Identity, scale=smb[:, S2C + m:S2C + m + 1]),
                deps=(tkB,))
            th = p.op('dve', lambda v, m=m: v.tensor_mul(
                hb[:, m * nt:(m + 1) * nt], t1b[:, :], t2b[:, :]), deps=(ta1, ta2))
            t1_sl.consumed(jt, th)
            h_last = th
            psfree[pA], psfree[pB] = ta1, ta2
        war['xnb'] = w1w2_last_mm

        g_last = [None]
        def ep_h1(m, psv, dep):
            tg = p.op('act', lambda s_, m=m, psv=psv: s_.activation(
                gb[:, m * nt:(m + 1) * nt], psv[:, :], AF.Gelu,
                scale=smb[:, H1C + m:H1C + m + 1]),
                deps=(dep,) + ((war.get('gb'),) if m == 0 else ()))
            g_last[0] = tg
            return tg
        h1_mm = emit_fm_linear(p, rhs=lambda k, s, e: hb[:, k * nt + s:k * nt + e],
                               rhs_tok=h_last, kchunks=32, mchunks=32, nt=nt,
                               slab_dram=h1s, slab_sb=slab, slabw=4096,
                               slab_sl=slab_sl, ps_ids=(0, 2), bufs=bufs,
                               psfree=psfree, epilogue=ep_h1)

        h2_last = [None]
        def ep_h2(m, psv, dep):
            tg = p.op('act', lambda s_, m=m, psv=psv: s_.activation(
                hb[:, m * nt:(m + 1) * nt], psv[:, :], AF.Gelu,
                scale=smb[:, H2C + m:H2C + m + 1]), deps=(dep,))
            h2_last[0] = tg
            return tg
        h2_mm = emit_fm_linear(p, rhs=lambda k, s, e: gb[:, k * nt + s:k * nt + e],
                               rhs_tok=g_last[0], kchunks=32, mchunks=32, nt=nt,
                               slab_dram=h2s, slab_sb=slab, slabw=4096,
                               slab_sl=slab_sl, ps_ids=(1, 3), bufs=bufs,
                               psfree=psfree, epilogue=ep_h2)
        war['gb'] = h2_mm

        def ep_w3(m, psv, dep):
            jt, wart = t1_sl.acquire()
            ty = p.op('act', lambda s_, m=m, psv=psv: s_.activation(
                t1b[:, :], psv[:, :], AF.Identity,
                scale=smb[:, W3C + m:W3C + m + 1]), deps=(dep,) + wart)
            jx, warx = bufs['xbuf_sl'].acquire()
            tx = p.dma('sp', xbuf[:, jx * nt:(jx + 1) * nt],
                       xd[m * 128:(m + 1) * 128, cols], deps=warx, key=('xbuf', jx))
            jo, waro = out_sl.acquire()
            to = p.op('dve', lambda v, jx=jx, jo=jo: v.tensor_add(
                outb[:, jo * nt:(jo + 1) * nt], t1b[:, :],
                xbuf[:, jx * nt:(jx + 1) * nt]), deps=(ty, tx) + waro)
            bufs['xbuf_sl'].consumed(jx, to)
            t1_sl.consumed(jt, to)
            ts = p.dma('sp', yd[m * 128:(m + 1) * 128, cols],
                       outb[:, jo * nt:(jo + 1) * nt], deps=(to,), key=('yst', jo))
            out_sl.consumed(jo, ts)
            return ty
        w3_mm = emit_fm_linear(p, rhs=lambda k, s, e: hb[:, k * nt + s:k * nt + e],
                               rhs_tok=h2_last[0], kchunks=32, mchunks=8, nt=nt,
                               slab_dram=w3s, slab_sb=slab, slabw=4096,
                               slab_sl=slab_sl, ps_ids=(0, 2), bufs=bufs,
                               psfree=psfree, epilogue=ep_w3)
        war['hb'] = w3_mm
        if TILE_BARRIER:
            p.barrier()

    p.final_dma_waits()
    p.emit()
    return nc


def l0ffn_host_inputs(w1_w, w1_s, w2_w, w2_s, h1_w, h1_s, h2_w, h2_s, w3_w, w3_s):
    sm = np.zeros((128, 136), np.float32)
    sm[:, 0:32] = w1_s.reshape(32, 128).T
    sm[:, 32:64] = w2_s.reshape(32, 128).T
    sm[:, 64:96] = h1_s.reshape(32, 128).T
    sm[:, 96:128] = h2_s.reshape(32, 128).T
    sm[:, 128:136] = w3_s.reshape(8, 128).T
    return {
        "w1s": prep_slabs(np.ascontiguousarray(w1_w.T)),
        "w2s": prep_slabs(np.ascontiguousarray(w2_w.T)),
        "h1s": prep_slabs(np.ascontiguousarray(h1_w.T)),
        "h2s": prep_slabs(np.ascontiguousarray(h2_w.T)),
        "w3s": prep_slabs(np.ascontiguousarray(w3_w.T)),
        "smalls": sm,
    }





def build_l1ffn(ntiles=8, nt=1024):
    nc = bass.Bass("TRN2", target_bir_lowering=False)
    ntok = ntiles * nt
    xd = nc.dram_tensor("x", [D, ntok], F32, kind="ExternalInput")
    w1e = nc.dram_tensor("w1e", [4, 32, 128, 1024], F32, kind="ExternalInput")
    w2e = nc.dram_tensor("w2e", [4, 8, 128, 4096], F32, kind="ExternalInput")
    wrp = nc.dram_tensor("wrp", [128, 32], F32, kind="ExternalInput")
    sm = nc.dram_tensor("smalls", [128, 160], F32, kind="ExternalInput")
    rowsd = nc.dram_tensor("rowsd", [1, 8], F32, kind="ExternalInput")
    seld = nc.dram_tensor("seld", [4, 512], F32, kind="ExternalInput")
    identf = nc.dram_tensor("identf", [128, 128], F32, kind="ExternalInput")
    yd = nc.dram_tensor("y", [D, ntok], F32, kind="ExternalOutput")
    dbgw = nc.dram_tensor("dbgw", [4, ntok], F32, kind="ExternalOutput") if DEBUG_S4 else None
    dbgc = nc.dram_tensor("dbgc", [128, 32], F32, kind="ExternalOutput") if DEBUG_S4 else None
    dbgh = nc.dram_tensor("dbgh", [128, ntok], F32, kind="ExternalOutput") if DEBUG_S4 else None
    dbga = nc.dram_tensor("dbga", [128, ntok], F32, kind="ExternalOutput") if DEBUG_S4 else None

    sb = nc.alloc_sbuf_tensor
    bufs = alloc_common(nc, nt, 160)
    hb = sb("hb", [128, 32 * nt], BF16)
    acc = sb("acc", [128, 8 * nt], F32)
    wbc = sb("wbc", [128, 4 * nt], F32)
    slab = sb("slab", [128, 2 * 4096], BF16)
    tmpb = sb("tmpb", [128, nt], F32)
    outb = sb("outb", [128, nt], F32)
    wrb = sb("wrb", [128, 32], F32)
    selb = sb("selb", [4, 512], F32)
    identb = sb("identb", [128, 128], F32)
    zb = sb("zb", [128, 4], F32)
    eb = sb("eb", [128, 4], F32)
    mb = sb("mb", [128, 4], F32)
    wbt = sb("wbt", [128, 4], F32)
    brc = sb("brc", [128, 4], F32)
    rw = sb("rw", [128, 3], F32)
    wrows = sb("wrows", [4, nt], F32)
    gb1 = sb("gb1", [128, 128], BF16)
    csb = sb("csb", [128, 32], F32)
    dsb = sb("dsb", [128, 32], F32)
    csum = sb("csum", [128, 8], F32)
    rowb = sb("rowb", [1, 8], F32)
    smb, xbuf, xnb = bufs['smb'], bufs['xbuf'], bufs['xnb']
    ps = bufs['psum']
    B1C, B2C = 0, 128

    p = Prog(nc)
    t_sm = p.dma('sp', smb[:, :], sm[:, :], key='smb')
    t_wr = p.dma('sp', wrb[:, :], wrp[:, :], key='wrb')
    t_sel = p.dma('sp', selb[:, :], seld[:, :], key='selb')
    t_id = p.dma('sp', identb[:, :], identf[:, :], key='identb')
    t_row = p.dma('sp', rowb[:, :], rowsd[:, :], key='rowb')
    t_of = p.op('pool', lambda g: g.memset(bufs['ones_f'][:, :], 1.0))
    t_o1 = p.op('pool', lambda g: g.memset(bufs['ones1'][:, :], 1.0))
    p.wait('pe', (t_of, t_o1, t_wr, t_sel, t_id, t_row))
    p.wait('act', (t_sm,))
    t_gb1 = p.op('act', lambda s: s.activation(gb1[:, :], smb[:, B1C:B1C + 128], AF.Gelu))
    t_brm = p.op('pe', lambda t: t.matmul(ps[0][:, 0:4], bufs['ones1'][0:1, :],
                                          rowb[0:1, 0:4], start=True, stop=True))
    t_brc = p.op('act', lambda s: s.activation(brc[:, :], ps[0][:, 0:4], AF.Identity),
                 deps=(t_brm,))

    slab_sl = Slots(2)
    out_sl = Slots(1)
    psfree = {0: t_brc}
    war = {}
    cd_tok = [None, None]

    for tile in range(ntiles):
        cols = slice(tile * nt, (tile + 1) * nt)

        rt_last = [None]
        def router(k, xnf_ap, tf, rt_last=rt_last):
            tl = None
            for c in range(nt // 128):
                first = (k == 0 and c == 0)
                tl = p.op('pe', lambda t, xnf_ap=xnf_ap, c=c, k=k: t.matmul(
                    ps[0][:, c * 4:(c + 1) * 4],
                    xnf_ap[:, c * 128:(c + 1) * 128],
                    wrb[:, k * 4:(k + 1) * 4],
                    start=(k == 0 and c == 0), stop=(k == 7 and c == nt // 128 - 1)),
                    deps=(tf, psfree.get(0)) if first else ((tf,) if c == 0 else ()),
                    tick=(c == nt // 128 - 1))
            rt_last[0] = tl
            return tl

        tok_xn = emit_layernorm(p, xd, cols, nt, bufs, psfree, war, router=router)

        tw_last = None
        for c in range(nt // 128):
            tz0 = p.op('act', lambda s, c=c: s.activation(
                zb[:, :], ps[0][:, c * 4:(c + 1) * 4], AF.Identity),
                deps=(rt_last[0], war.get('zb')))
            psfree[0] = tz0
            tz = p.op('dve', lambda v: v.tensor_add(zb[:, :], zb[:, :], brc[:, :]),
                      deps=(tz0, t_brc))
            tm_toks = []
            for e in range(4):
                tr = None
                first = True
                for e2 in range(4):
                    if e2 == e:
                        continue
                    op = ALU.is_ge if e2 < e else ALU.is_gt
                    if first:
                        tr = p.op('dve', lambda v, e2=e2, e=e, op=op: v.tensor_tensor(
                            rw[:, 0:1], zb[:, e2:e2 + 1], zb[:, e:e + 1], op),
                            deps=(tz,))
                        first = False
                    else:
                        t_c = p.op('dve', lambda v, e2=e2, e=e, op=op: v.tensor_tensor(
                            rw[:, 1:2], zb[:, e2:e2 + 1], zb[:, e:e + 1], op),
                            deps=(tr,))
                        tr = p.op('dve', lambda v: v.tensor_add(
                            rw[:, 0:1], rw[:, 0:1], rw[:, 1:2]), deps=(t_c,))
                tmk = p.op('dve', lambda v, e=e: v.tensor_single_scalar(
                    mb[:, e:e + 1], rw[:, 0:1], 1.0, ALU.is_le), deps=(tr,))
                tm_toks.append(tmk)
            tnm = p.op('dve', lambda v: v.reduce_max(
                rw[:, 0:1], zb[:, :], axis=AX.X, negate=True), deps=tuple(tm_toks))
            tex = p.op('act', lambda s: s.activation(
                eb[:, :], zb[:, :], AF.Exp, bias=rw[:, 0:1]), deps=(tnm,))
            war['zb'] = tex
            tebm = p.op('dve', lambda v: v.tensor_mul(eb[:, :], eb[:, :], mb[:, :]),
                        deps=(tex,))
            tss = p.op('dve', lambda v: v.reduce_sum(rw[:, 1:2], eb[:, :], axis=AX.X),
                       deps=(tebm,))
            trc2 = p.op('dve', lambda v: v.reciprocal(rw[:, 2:3], rw[:, 1:2]), deps=(tss,))
            twb = p.op('dve', lambda v: v.tensor_scalar(
                wbt[:, :], eb[:, :], rw[:, 2:3], None, ALU.mult),
                deps=(trc2, war.get('wbt')))
            ttr = p.op('pe', lambda t: t.transpose(
                ps[1][0:4, 0:128], wbt[:, :], identb[:, :]),
                deps=(twb, psfree.get(1)))
            war['wbt'] = ttr
            twr = p.op('act', lambda s, c=c: s.activation(
                wrows[0:4, c * 128:(c + 1) * 128], ps[1][0:4, 0:128], AF.Identity),
                deps=(ttr,))
            psfree[1] = twr
            tw_last = twr

        if DEBUG_S4:
            p.dma('sp', dbgw[:, cols], wrows[0:4, 0:nt], deps=(tw_last,), key='dbgw')
        tok_wbc = None
        for e in range(4):
            pid = 2 + (e % 2)
            tb = None
            for hh in range(nt // 512):
                s, en = hh * 512, (hh + 1) * 512
                tb = p.op('pe', lambda t, e=e, s=s, en=en, pid=pid: t.matmul(
                    ps[pid][:, s:en], selb[0:4, e * 128:(e + 1) * 128],
                    wrows[0:4, s:en], start=True, stop=True),
                    deps=(tw_last, psfree.get(pid), war.get('wbc')) if hh == 0 else (),
                    tick=(hh == nt // 512 - 1))
            te = p.op('act', lambda s_, e=e, pid=pid: s_.activation(
                wbc[:, e * nt:(e + 1) * nt], ps[pid][:, :], AF.Identity), deps=(tb,))
            psfree[pid] = te
            tok_wbc = te

        for e in range(4):
            h_last = None
            for m in range(32):
                par = m % 2
                pid = par
                j, warj = slab_sl.acquire()
                td = p.dma('pool', slab[:, j * 4096:j * 4096 + 1024], w1e[e, m],
                           deps=warj, key=('slab', j))
                sv = slab[:, j * 4096:j * 4096 + 1024]
                bdep = psfree.get(pid)
                tk = None
                for k in range(8):
                    for hh in range(nt // 512):
                        s, en = hh * 512, (hh + 1) * 512
                        tk = p.op('pe', lambda t, sv=sv, k=k, s=s, en=en, pid=pid: t.matmul(
                            ps[pid][:, s:en], sv[:, k * 128:(k + 1) * 128],
                            xnb[:, k * nt + s:k * nt + en],
                            start=(k == 0), stop=(k == 7)),
                            deps=(td, tok_xn, bdep, war.get('hb')) if (k == 0 and hh == 0) else (),
                            tick=(k == 7 and hh == nt // 512 - 1))
                slab_sl.consumed(j, tk)
                tg = p.op('act', lambda s_, m=m, e=e, pid=pid: s_.activation(
                    hb[:, m * nt:(m + 1) * nt], ps[pid][:, :], AF.Gelu,
                    bias=smb[:, B1C + e * 32 + m:B1C + e * 32 + m + 1]),
                    deps=(tk,))
                psfree[pid] = tg
                h_last = tg
                if DEBUG_S4 and e == 1 and m == 0:
                    th32 = p.op('dve', lambda v: v.tensor_copy(tmpb[:, :], hb[:, 0:nt]),
                                deps=(tg, war.get('tmpb')))
                    war['tmpb'] = th32
                    p.dma('sp', dbgh[:, cols], tmpb[:, :], deps=(th32,), key='dbgh')

            ye_last = None
            for m in range(8):
                par = m % 2
                pid = 2 + par
                j, warj = slab_sl.acquire()
                td = p.dma('pool', slab[:, j * 4096:j * 4096 + 4096], w2e[e, m],
                           deps=warj, key=('slab', j))
                sv = slab[:, j * 4096:j * 4096 + 4096]
                bdep = psfree.get(pid)
                tk = None
                for k in range(32):
                    for hh in range(nt // 512):
                        s, en = hh * 512, (hh + 1) * 512
                        tk = p.op('pe', lambda t, sv=sv, k=k, s=s, en=en, pid=pid: t.matmul(
                            ps[pid][:, s:en], sv[:, k * 128:(k + 1) * 128],
                            hb[:, k * nt + s:k * nt + en],
                            start=(k == 0), stop=(k == 31)),
                            deps=(td, h_last, bdep) if (k == 0 and hh == 0) else (),
                            tick=(k == 31 and hh == nt // 512 - 1))
                ye_last = tk
                tkc = None
                if tile == 0:
                    for k in range(32):
                        tkc = p.op('pe', lambda t, sv=sv, k=k, m=m, e=e: t.matmul(
                            ps[1][:, 4 + m:5 + m], sv[:, k * 128:(k + 1) * 128],
                            gb1[:, e * 32 + k:e * 32 + k + 1], start=(k == 0), stop=(k == 31)),
                            deps=((t_gb1, psfree.get(1)) if (k == 0 and m == 0) else ()),
                            tick=(k == 31))
                    slab_sl.consumed(j, tkc)
                slab_sl.consumed(j, tk)
                if e == 0:
                    tt = p.op('dve', lambda v, m=m, pid=pid: v.tensor_mul(
                        acc[:, m * nt:(m + 1) * nt], ps[pid][:, :],
                        wbc[:, 0:nt]),
                        deps=(tk, tok_wbc, war.get(('acc', m))))
                else:
                    t0 = p.op('dve', lambda v, pid=pid, e=e: v.tensor_mul(
                        tmpb[:, :], ps[pid][:, :], wbc[:, e * nt:(e + 1) * nt]),
                        deps=(tk, war.get('tmpb')))
                    tt = p.op('dve', lambda v, m=m: v.tensor_add(
                        acc[:, m * nt:(m + 1) * nt], acc[:, m * nt:(m + 1) * nt],
                        tmpb[:, :]), deps=(t0,))
                    war['tmpb'] = tt
                war[('acc_w', m)] = tt
                psfree[pid] = tt
                if DEBUG_S4 and e == 3 and m == 0:
                    p.dma('sp', dbga[:, cols], acc[:, 0:nt], deps=(tt,), key='dbga')
            war['hb'] = ye_last
            if tile == 0:
                tev = p.op('act', lambda s, e=e: s.activation(
                    csb[:, e * 8:(e + 1) * 8], ps[1][:, 4:12], AF.Identity),
                    deps=(tkc,))
                psfree[1] = tev
                cd_tok[0] = tev
        war['xnb'] = ye_last
        war['xnf'] = ye_last
        war['wbc'] = tt

        if tile == 0:
            tcd = cd_tok[0]
            td1 = p.op('dve', lambda v: v.tensor_scalar_mul(
                dsb[:, :], csb[:, :], -1.0), deps=(tcd,))
            tc3 = p.op('dve', lambda v: v.tensor_add(csum[:, :], csb[:, 0:8], csb[:, 8:16]),
                       deps=(tcd,))
            for blk in (csb[:, 16:24], csb[:, 24:32], smb[:, B2C:B2C + 8],
                        smb[:, B2C + 8:B2C + 16], smb[:, B2C + 16:B2C + 24],
                        smb[:, B2C + 24:B2C + 32]):
                tc3 = p.op('dve', lambda v, blk=blk: v.tensor_add(
                    csum[:, :], csum[:, :], blk), deps=(tc3,))
            cd_tok[1] = (td1, tc3)
            if DEBUG_S4:
                p.dma('sp', dbgc[:, :], csb[:, :], deps=(cd_tok[0],), key='dbgc')

        td1, tc3 = cd_tok[1]
        for m in range(8):
            tprev = war[('acc_w', m)]
            for e in range(4):
                tprev = p.op('dve', lambda v, m=m, e=e: v.scalar_tensor_tensor(
                    acc[:, m * nt:(m + 1) * nt], wbc[:, e * nt:(e + 1) * nt],
                    dsb[:, e * 8 + m:e * 8 + m + 1], acc[:, m * nt:(m + 1) * nt],
                    ALU.mult, ALU.add), deps=(tprev, td1))
            tcs = p.op('dve', lambda v, m=m: v.tensor_scalar(
                acc[:, m * nt:(m + 1) * nt], acc[:, m * nt:(m + 1) * nt],
                csum[:, m:m + 1], None, ALU.add), deps=(tprev, tc3))
            jx, warx = bufs['xbuf_sl'].acquire()
            tx = p.dma('sp', xbuf[:, jx * nt:(jx + 1) * nt],
                       xd[m * 128:(m + 1) * 128, cols], deps=warx, key=('xbuf', jx))
            jo, waro = out_sl.acquire()
            to = p.op('dve', lambda v, m=m, jx=jx: v.tensor_add(
                outb[:, :], acc[:, m * nt:(m + 1) * nt],
                xbuf[:, jx * nt:(jx + 1) * nt]), deps=(tcs, tx) + waro)
            bufs['xbuf_sl'].consumed(jx, to)
            war[('acc', m)] = to
            ts = p.dma('sp', yd[m * 128:(m + 1) * 128, cols], outb[:, :],
                       deps=(to,), key='yst')
            out_sl.consumed(jo, ts)

    p.final_dma_waits()
    p.emit()
    return nc


def l1ffn_host_inputs(w1, b1, w2, b2, wr, br):
    w1s = np.stack([prep_slabs(np.ascontiguousarray(w1[e])) for e in range(4)])
    w2s = np.stack([prep_slabs(np.ascontiguousarray(w2[e])) for e in range(4)])
    wrp = np.zeros((128, 32), np.float32)
    for k in range(8):
        wrp[:, k * 4:(k + 1) * 4] = wr[k * 128:(k + 1) * 128, :]
    sm = np.zeros((128, 160), np.float32)
    for e in range(4):
        sm[:, e * 32:(e + 1) * 32] = b1[e].reshape(32, 128).T
        sm[:, 128 + e * 8:128 + (e + 1) * 8] = b2[e].reshape(8, 128).T
    rows = np.zeros((1, 8), np.float32)
    rows[0, 0:4] = br
    sel = np.zeros((4, 512), np.float32)
    for e in range(4):
        sel[e, e * 128:(e + 1) * 128] = 1.0
    return {"w1e": w1s, "w2e": w2s, "wrp": wrp, "smalls": sm, "rowsd": rows,
            "seld": sel, "identf": np.eye(128, dtype=np.float32)}



def build_l1attn(ngroups=4, tslice=None):
    nc = bass.Bass("TRN2", target_bir_lowering=False)
    ts_ = tslice if tslice is not None else ngroups * 128
    ntok = BATCH * ts_
    nt = 128
    xd = nc.dram_tensor("x", [D, ntok], F32, kind="ExternalInput")
    wqd = nc.dram_tensor("wq", [128, 8192], F32, kind="ExternalInput")
    wkd = nc.dram_tensor("wk", [128, 8192], F32, kind="ExternalInput")
    wvd = nc.dram_tensor("wv", [128, 8192], F32, kind="ExternalInput")
    rowd = nc.dram_tensor("rows", [1, 3072], F32, kind="ExternalInput")
    identd = nc.dram_tensor("identd", [128, 128], F32, kind="ExternalInput")
    yd = nc.dram_tensor("y", [D, ntok], F32, kind="ExternalOutput")

    sb = nc.alloc_sbuf_tensor
    bufs = alloc_common(nc, nt, 8, with_psum=False)
    LN01 = nc.alloc_psum_tensor("LN01", [128, 512], F32)
    LN23 = nc.alloc_psum_tensor("LN23", [128, 512], F32)
    bufs['psum'] = [LN01[:, 0:128], LN23[:, 0:128], LN01[:, 128:256], LN23[:, 128:256]]
    qb = sb("qb", [128, 16 * 1024], BF16)
    kb = sb("kb", [128, 16 * 1024], BF16)
    vb = sb("vb", [128, 16 * 1024], BF16)
    wqs = sb("wqs", [128, 8192], BF16)
    wks = sb("wks", [128, 8192], BF16)
    wvs = sb("wvs", [128, 8192], BF16)
    rowb = sb("rowb", [1, 3072], BF16)
    identb = sb("identb", [128, 128], BF16)
    ssc = sb("ssc", [128, 256], F32)
    ebx = sb("ebx", [128, 256], F32)
    abx = sb("abx", [128, 256], F32)
    ptmp = sb("ptmp", [128, 1024], F32)
    ob = sb("ob", [128, 1024], F32)
    obb = sb("obb", [128, 1024], BF16)
    nmb = sb("nmb", [128, 16], F32)
    smsb = sb("smsb", [128, 16], F32)
    rsb = sb("rsb", [128, 16], F32)
    outs = sb("outs", [128, 2 * 128], F32)
    xres = sb("xres", [128, 2 * 128], F32)
    ps = bufs['psum']
    P0 = nc.alloc_psum_tensor("P0", [128, 1024], F32)
    P1 = nc.alloc_psum_tensor("P1", [128, 1024], F32)
    PT = nc.alloc_psum_tensor("PT", [128, 128], BF16)

    p = Prog(nc)
    t_wq = p.dma('pool', wqs[:, :], wqd[:, :], key='wq')
    t_wk = p.dma('pool', wks[:, :], wkd[:, :], key='wk')
    t_wv = p.dma('pool', wvs[:, :], wvd[:, :], key='wv')
    t_rw = p.dma('pool', rowb[:, :], rowd[:, :], key='rows')
    t_idn = p.dma('pool', identb[:, :], identd[:, :], key='ident')
    t_of = p.op('pool', lambda g: g.memset(bufs['ones_f'][:, :], 1.0))
    t_o1 = p.op('pool', lambda g: g.memset(bufs['ones1'][:, :], 1.0))
    t_ob1 = p.op('pool', lambda g: g.memset(bufs['onesb'][:, :], 1.0))
    p.wait('pe', (t_of, t_o1, t_ob1, t_wq, t_wk, t_wv, t_rw, t_idn))

    psfree = {}
    war = {}
    pT = {'q': P0, 'k': P1, 'v': P0}
    wsb = {'q': wqs, 'k': wks, 'v': wvs}
    outbuf = {'q': qb, 'k': kb, 'v': vb}
    pfkey = {'q': 'P0', 'k': 'P1', 'v': 'P0'}
    xres_sl = Slots(2)
    outs_sl = Slots(2)

    for g in range(ngroups):
        for b in range(16):
            cols = slice(b * ts_ + g * 128, b * ts_ + (g + 1) * 128)
            tok_xn = emit_layernorm(p, xd, cols, nt, bufs, psfree, war)
            psfree[0], psfree[1] = psfree.get(2), psfree.get(3)
            for w in ('q', 'k', 'v'):
                P = pT[w]
                pk = pfkey[w]
                tk = None
                for k in range(8):
                    for hh in range(2):
                        s, e = hh * 512, (hh + 1) * 512
                        tk = p.op('pe', lambda t, k=k, s=s, e=e, P=P, w=w: t.matmul(
                            P[:, s:e], bufs['xnb'][:, k * nt:(k + 1) * nt],
                            wsb[w][:, k * 1024 + s:k * 1024 + e],
                            start=(k == 0), stop=False),
                            deps=(tok_xn, psfree.get(pk)) if (k == 0 and hh == 0) else (),
                            tick=False)
                widx = ('q', 'k', 'v').index(w)
                for hh in range(2):
                    s, e = hh * 512, (hh + 1) * 512
                    tk = p.op('pe', lambda t, s=s, e=e, P=P, widx=widx: t.matmul(
                        P[:, s:e], bufs['onesb'][0:1, :],
                        rowb[0:1, widx * 1024 + s:widx * 1024 + e],
                        start=False, stop=True), tick=(hh == 1))
                tev = p.op('act', lambda s_, b=b, P=P, w=w: s_.activation(
                    outbuf[w][:, b * 1024:(b + 1) * 1024], P[:, :], AF.Identity,
                    scale=0.125 if w == 'q' else 1.0),
                    deps=(tk,) + ((war.get('qkv'),) if (b == 0 and w == 'q') else ()))
                psfree[pk] = tev
        tok_qkv = tev

        t_last = None
        for i in range(16):
            qv = qb[:, i * 1024:(i + 1) * 1024]
            tred = None
            for j in range(i + 1):
                tp_ = p.op('dve', lambda v, j=j, qv=qv: v.tensor_mul(
                    ptmp[:, :], qv, kb[:, j * 1024:(j + 1) * 1024]),
                    deps=(tok_qkv, war.get('ptmp')))
                p3 = ptmp[:, :].rearrange("p (h d) -> p h d", h=16)
                s3 = ssc[:, :].rearrange("p (h j) -> p h j", j=16)
                tred = p.op('dve', lambda v, j=j, p3=p3, s3=s3: v.reduce_sum(
                    s3[:, :, j], p3, axis=AX.X), deps=(tp_,))
                war['ptmp'] = tred
            s3 = ssc[:, :].rearrange("p (h j) -> p h j", j=16)
            e3 = ebx[:, :].rearrange("p (h j) -> p h j", j=16)
            a3 = abx[:, :].rearrange("p (h j) -> p h j", j=16)
            tnm = p.op('dve', lambda v, i=i, s3=s3: v.reduce_max(
                nmb[:, :], s3[:, :, 0:i + 1], axis=AX.X, negate=True), deps=(tred,))
            tsub = p.op('dve', lambda v, i=i, s3=s3, e3=e3: v.tensor_add(
                e3[:, :, 0:i + 1], s3[:, :, 0:i + 1],
                nmb[:, :].broadcast_to([128, 16, i + 1])),
                deps=(tnm, war.get('ebx')))
            tex = p.op('act', lambda s_, i=i, e3=e3: s_.activation(
                e3[:, :, 0:i + 1], e3[:, :, 0:i + 1], AF.Exp), deps=(tsub,))
            tsm = p.op('dve', lambda v, i=i, e3=e3: v.reduce_sum(
                smsb[:, :], e3[:, :, 0:i + 1], axis=AX.X), deps=(tex,))
            tp1 = p.op('dve', lambda v: v.tensor_scalar_add(smsb[:, :], smsb[:, :], 1.0),
                       deps=(tsm,))
            trc = p.op('dve', lambda v: v.reciprocal(rsb[:, :], smsb[:, :]), deps=(tp1,))
            taw = p.op('dve', lambda v, i=i, e3=e3, a3=a3: v.tensor_mul(
                a3[:, :, 0:i + 1], e3[:, :, 0:i + 1],
                rsb[:, :].broadcast_to([128, 16, i + 1])),
                deps=(trc, war.get('abx')))
            war['ebx'] = tsm

            tacc = None
            for j in range(i + 1):
                for h in range(16):
                    vsl = vb[:, j * 1024 + h * 64:j * 1024 + (h + 1) * 64]
                    osl = ob[:, h * 64:(h + 1) * 64]
                    asl = abx[:, h * 16 + j:h * 16 + j + 1]
                    if j == 0:
                        tacc = p.op('dve', lambda v, vsl=vsl, osl=osl, asl=asl: v.tensor_scalar(
                            osl, vsl, asl, None, ALU.mult),
                            deps=(taw,) if h == 0 else ())
                    else:
                        tacc = p.op('dve', lambda v, vsl=vsl, osl=osl, asl=asl: v.scalar_tensor_tensor(
                            osl, vsl, asl, osl, ALU.mult, ALU.add),
                            deps=())
            war['abx'] = tacc
            tcb = p.op('act', lambda s_: s_.activation(obb[:, :], ob[:, :], AF.Identity),
                       deps=(tacc, war.get('obb')))
            ttr_last = None
            for m in range(8):
                colsb = slice(i * ts_ + g * 128, i * ts_ + (g + 1) * 128)
                jx, warx = xres_sl.acquire()
                tx = p.dma('sp', xres[:, jx * 128:(jx + 1) * 128],
                           xd[m * 128:(m + 1) * 128, colsb], deps=warx, key=('xres', jx))
                ttr = p.op('pe', lambda t, m=m: t.transpose(
                    PT[:, :], obb[:, m * 128:(m + 1) * 128], identb[:, :]),
                    deps=(tcb, psfree.get('PT')))
                jo, waro = outs_sl.acquire()
                tad = p.op('dve', lambda v, jo=jo, jx=jx: v.tensor_add(
                    outs[:, jo * 128:(jo + 1) * 128], PT[:, :],
                    xres[:, jx * 128:(jx + 1) * 128]), deps=(ttr, tx) + waro)
                psfree['PT'] = tad
                xres_sl.consumed(jx, tad)
                tst = p.dma('sp', yd[m * 128:(m + 1) * 128, colsb],
                            outs[:, jo * 128:(jo + 1) * 128], deps=(tad,),
                            key=('yst', jo))
                outs_sl.consumed(jo, tst)
                ttr_last = ttr
            war['obb'] = ttr_last
            t_last = tacc
        war['qkv'] = t_last

    p.final_dma_waits()
    p.emit()
    return nc


def l1attn_host_inputs(Wq, bq, Wk, bk, Wv, bv):
    def prep(w):
        wt = np.ascontiguousarray(w.T).reshape(8, 128, 1024).transpose(1, 0, 2)
        return np.ascontiguousarray(wt.reshape(128, 8192))
    rows = np.zeros((1, 3072), np.float32)
    rows[0, 0:1024] = bq
    rows[0, 1024:2048] = bk
    rows[0, 2048:3072] = bv
    return {"wq": prep(Wq), "wk": prep(Wk), "wv": prep(Wv), "rows": rows,
            "identd": np.eye(128, dtype=np.float32)}



def build_l0attn_a(tslice=TSLICE):
    nc = bass.Bass("TRN2", target_bir_lowering=False)
    nt = tslice
    ntok = BATCH * nt
    xd = nc.dram_tensor("x", [D, ntok], F32, kind="ExternalInput")
    wqs = nc.dram_tensor("wqs", [8, 128, 1024], F32, kind="ExternalInput")
    wks = nc.dram_tensor("wks", [8, 128, 1024], F32, kind="ExternalInput")
    wvp = nc.dram_tensor("wvp", [128, 8192], F32, kind="ExternalInput")
    identd = nc.dram_tensor("identd", [128, 128], F32, kind="ExternalInput")
    qfd = nc.dram_tensor("qf", [BATCH, 1024, nt], BF16, kind="ExternalOutput")
    kfd = nc.dram_tensor("kf", [BATCH, 1024, nt], BF16, kind="ExternalOutput")
    ktmd = nc.dram_tensor("ktm", [BATCH, nt, 1024], BF16, kind="ExternalOutput")
    vexd = nc.dram_tensor("vex", [BATCH, nt, 1040], BF16, kind="ExternalOutput")
    cstd = nc.dram_tensor("cst", [BATCH * 128, 520], F32, kind="ExternalOutput")

    ntc = nt // 128
    sb = nc.alloc_sbuf_tensor
    bufs = alloc_common(nc, nt, 8, with_psum=False)
    LNA = nc.alloc_psum_tensor("LNA", [128, nt], F32)
    LNB = nc.alloc_psum_tensor("LNB", [128, nt], F32)
    bufs['psum'] = [LNA, LNB, LNA, LNB]
    psA = nc.alloc_psum_tensor("psA", [128, 512], F32)
    psB = nc.alloc_psum_tensor("psB", [128, 512], F32)
    psV = nc.alloc_psum_tensor("psV", [128, 512], F32)
    PT = nc.alloc_psum_tensor("PT", [128, 128], BF16)
    psS = nc.alloc_psum_tensor("psS", [128, 1024], F32)

    wqsb = sb("wqsb", [128, 8 * 1024], BF16)
    wksb = sb("wksb", [128, 8 * 1024], BF16)
    wvsb = sb("wvsb", [128, 8192], BF16)
    identb = sb("identb", [128, 128], BF16)
    t1b = sb("t1b", [128, 512], F32)
    t2b = sb("t2b", [128, 512], F32)
    qfsb = sb("qfsb", [128, 8 * nt], BF16)
    kfsb = sb("kfsb", [128, 8 * nt], BF16)
    vexsb = sb("vexsb", [128, ntc * 1040], BF16)
    ktmsb = sb("ktmsb", [128, ntc * 1024], BF16)
    cstsb = sb("cstsb", [128, 520], F32)

    p = Prog(nc)
    t_wq = p.dma('pool', wqsb[:, :].rearrange("p (m f) -> p m f", m=8),
                 wqs[:, :, :].rearrange("m p f -> p m f"), key='wq')
    t_wk = p.dma('pool', wksb[:, :].rearrange("p (m f) -> p m f", m=8),
                 wks[:, :, :].rearrange("m p f -> p m f"), key='wk')
    t_wv = p.dma('pool', wvsb[:, :], wvp[:, :], key='wv')
    t_idn = p.dma('pool', identb[:, :], identd[:, :], key='ident')
    t_of = p.op('pool', lambda g: g.memset(bufs['ones_f'][:, :], 1.0))
    t_o1 = p.op('pool', lambda g: g.memset(bufs['ones1'][:, :], 1.0))
    p.wait('pe', (t_of, t_o1, t_wq, t_wk, t_wv, t_idn))

    psfree = {}
    war = {}
    qf_sl, kf_sl, kt_sl, vx_sl, cs_sl = Slots(1), Slots(1), Slots(1), Slots(1), Slots(1)

    def feat_epilogue(pssrc, dst, dstsl, dep):
        te = p.op('act', lambda s: s.activation(t1b[:, :], pssrc[:, :], AF.Exp),
                  deps=(dep, war.get('t1b')))
        tm_ = p.op('dve', lambda v: v.tensor_scalar_min(t1b[:, :], t1b[:, :], 1.0),
                   deps=(te,))
        tr = p.op('dve', lambda v: v.tensor_scalar_max(t2b[:, :], pssrc[:, :], 0.0),
                  deps=(dep, war.get('t2b')))
        ta = p.op('dve', lambda v: v.tensor_add(dstsl, t1b[:, :], t2b[:, :]),
                  deps=(tm_, tr))
        war['t1b'] = ta
        war['t2b'] = ta
        return ta, tr

    for b in range(16):
        cols = slice(b * nt, (b + 1) * nt)
        tok_xn = emit_layernorm(p, xd, cols, nt, bufs, psfree, war)
        psfree[0], psfree[1] = psfree.get(2), psfree.get(3)
        xnb = bufs['xnb']

        for (wsb_, outsb, osl, od, dk) in ((wqsb, qfsb, qf_sl, qfd, 'qf'),
                                           (wksb, kfsb, kf_sl, kfd, 'kf')):
            _, warq = osl.acquire()
            last = None
            for m in range(8):
                P = psA if m % 2 == 0 else psB
                pk = 'A' if m % 2 == 0 else 'B'
                tk = None
                for k in range(8):
                    tk = p.op('pe', lambda t, m=m, k=k, P=P, wsb_=wsb_: t.matmul(
                        P[:, 0:nt],
                        wsb_[:, m * 1024 + k * 128:m * 1024 + (k + 1) * 128],
                        xnb[:, k * nt:(k + 1) * nt],
                        start=(k == 0), stop=(k == 7)),
                        deps=(tok_xn, psfree.get(pk)) if k == 0 else (),
                        tick=(k == 7))
                tv, tpr = feat_epilogue(P, None, outsb[:, m * nt:(m + 1) * nt], tk)
                psfree[pk] = tpr
                last = tv
            ts = p.dma('sp', od[b].rearrange("(m p) t -> p m t", m=8),
                       outsb[:, :].rearrange("p (m t) -> p m t", m=8),
                       deps=(last,) + warq, key=dk)
            osl.consumed(0, ts)
            if wsb_ is wqsb:
                tok_qf = last
            else:
                tok_kf = last

        _, warv = vx_sl.acquire()
        tvm = p.op('pool', lambda g: g.memset(vexsb[:, :], 1.0), deps=warv)
        v_evac = None
        for tc in range(ntc):
            for half in range(2):
                tk = None
                for k in range(8):
                    tk = p.op('pe', lambda t, tc=tc, half=half, k=k: t.matmul(
                        psV[:, 0:512],
                        xnb[:, k * nt + tc * 128:k * nt + (tc + 1) * 128],
                        wvsb[:, k * 1024 + half * 512:k * 1024 + (half + 1) * 512],
                        start=(k == 0), stop=(k == 7)),
                        deps=(tok_xn, psfree.get('V')) if k == 0 else (),
                        tick=(k == 7))
                for hh in range(8):
                    h = half * 8 + hh
                    v_evac = p.op('act', lambda s, tc=tc, h=h, hh=hh: s.activation(
                        vexsb[:, tc * 1040 + h * 65:tc * 1040 + h * 65 + 64],
                        psV[:, hh * 64:(hh + 1) * 64], AF.Identity),
                        deps=(tk, tvm) if hh == 0 else (tk,))
                psfree['V'] = v_evac
        tsv = p.dma('sp', vexd[b].rearrange("(c p) f -> p c f", c=ntc),
                    vexsb[:, :].rearrange("p (c f) -> p c f", c=ntc),
                    deps=(v_evac,), key='vex')
        vx_sl.consumed(0, tsv)

        _, warkt = kt_sl.acquire()
        kt_evac = None
        for tc in range(ntc):
            for m in range(8):
                ttr = p.op('pe', lambda t, tc=tc, m=m: t.transpose(
                    PT[:, :], kfsb[:, m * nt + tc * 128:m * nt + (tc + 1) * 128],
                    identb[:, :]),
                    deps=(tok_kf, psfree.get('PT')) if (tc == 0 and m == 0) else
                         (psfree.get('PT'),))
                kt_evac = p.op('act', lambda s, tc=tc, m=m: s.activation(
                    ktmsb[:, tc * 1024 + m * 128:tc * 1024 + (m + 1) * 128],
                    PT[:, :], AF.Identity),
                    deps=(ttr,) + (warkt if (tc == 0 and m == 0) else ()))
                psfree['PT'] = kt_evac
        tskt = p.dma('sp', ktmd[b].rearrange("(c p) f -> p c f", c=ntc),
                     ktmsb[:, :].rearrange("p (c f) -> p c f", c=ntc),
                     deps=(kt_evac,), key='ktm')
        kt_sl.consumed(0, tskt)

        st_last = None
        for h in range(16):
            q_ = h // 2
            colbase = q_ * 65 if q_ < 7 else 512
            rowbase = (h % 2) * 64
            tk = None
            for tc in range(ntc):
                tk = p.op('pe', lambda t, h=h, tc=tc, colbase=colbase, rowbase=rowbase: t.matmul(
                    psS[rowbase:rowbase + 64, colbase:colbase + 65],
                    ktmsb[:, tc * 1024 + h * 64:tc * 1024 + (h + 1) * 64],
                    vexsb[:, tc * 1040 + h * 65:tc * 1040 + (h + 1) * 65],
                    start=(tc == 0), stop=(tc == ntc - 1)),
                    deps=(kt_evac, v_evac, psfree.get('S')) if (h == 0 and tc == 0) else (),
                    tick=(tc == ntc - 1))
            st_last = tk
        cs_evac = None
        _, warcs = cs_sl.acquire()
        for h in range(16):
            q_ = h // 2
            colbase = q_ * 65 if q_ < 7 else 512
            rowbase = (h % 2) * 64
            cs_evac = p.op('act', lambda s, h=h, colbase=colbase, rowbase=rowbase: s.activation(
                cstsb[rowbase:rowbase + 64, (h // 2) * 65:(h // 2) * 65 + 65],
                psS[rowbase:rowbase + 64, colbase:colbase + 65], AF.Identity),
                deps=(st_last,) + (warcs if h == 0 else ()))
        psfree['S'] = cs_evac
        tscs = p.dma('sp', cstd[b * 128:(b + 1) * 128, :], cstsb[:, :],
                     deps=(cs_evac,), key='cst')
        cs_sl.consumed(0, tscs)
        war['xnb'] = st_last
        qf_sl.consumed(0, st_last)
        kf_sl.consumed(0, kt_evac)
        vx_sl.consumed(0, st_last)
        kt_sl.consumed(0, st_last)

    p.final_dma_waits()
    p.emit()
    return nc


def l0attn_a_host_inputs(Wq, Wk, Wv):
    wvp = np.ascontiguousarray(
        Wv.reshape(8, 128, 1024).transpose(1, 0, 2).reshape(128, 8192))
    return {"wqs": prep_slabs(Wq), "wks": prep_slabs(Wk), "wvp": wvp,
            "identd": np.eye(128, dtype=np.float32)}



def build_l0attn_b(tslice=TSLICE):
    nc = bass.Bass("TRN2", target_bir_lowering=False)
    nt = tslice
    ntc = nt // 128
    ntok = BATCH * nt
    xd = nc.dram_tensor("x", [D, ntok], F32, kind="ExternalInput")
    qfd = nc.dram_tensor("qf", [BATCH, 1024, nt], BF16, kind="ExternalInput")
    kfd = nc.dram_tensor("kf", [BATCH, 1024, nt], BF16, kind="ExternalInput")
    ktmd = nc.dram_tensor("ktm", [BATCH, nt, 1024], BF16, kind="ExternalInput")
    vexd = nc.dram_tensor("vex", [BATCH, nt, 1040], BF16, kind="ExternalInput")
    callc = nc.dram_tensor("callc", [NCORES * BATCH * 128, 520], F32, kind="ExternalInput")
    rmaskd = nc.dram_tensor("rmask", [128, 8], F32, kind="ExternalInput")
    wos = nc.dram_tensor("wos", [8, 128, 1024], F32, kind="ExternalInput")
    identd = nc.dram_tensor("identd", [128, 128], F32, kind="ExternalInput")
    umaskd = nc.dram_tensor("umaskd", [128, 128], F32, kind="ExternalInput")
    yd = nc.dram_tensor("y", [D, ntok], F32, kind="ExternalOutput")

    sb = nc.alloc_sbuf_tensor
    qsb = sb("qsb", [128, 2 * 8 * nt], BF16)
    ksb = sb("ksb", [128, 2 * 8 * nt], BF16)
    ktsb = sb("ktsb", [128, 2 * ntc * 1024], BF16)
    vxsb = sb("vxsb", [128, 2 * ntc * 1040], BF16)
    sf = sb("sf", [128, 16 * 520], F32)
    sbf = sb("sbf", [128, 520], BF16)
    csl = sb("csl", [128, 2 * 520], F32)
    rmb = sb("rmb", [128, 8], F32)
    wosb = sb("wosb", [128, 8 * 1024], BF16)
    identb = sb("identb", [128, 128], BF16)
    umaskb = sb("umaskb", [128, 128], BF16)
    amt = sb("amt", [128, 128], BF16)
    denb = sb("denb", [128, 1], F32)
    ytm = sb("ytm", [128, 1024], BF16)
    yfm = sb("yfm", [128, 8 * nt], BF16)
    t1b = sb("t1b", [128, 512], F32)
    xbuf = sb("xbuf", [128, 2 * 512], F32)
    outb = sb("outb", [128, 2 * 512], F32)

    psAt = nc.alloc_psum_tensor("psAt", [128, 128], F32)
    psN = nc.alloc_psum_tensor("psN", [128, 512], F32)
    psU = nc.alloc_psum_tensor("psU", [128, 512], F32)
    PT = nc.alloc_psum_tensor("PT", [128, 128], BF16)
    psO0 = nc.alloc_psum_tensor("psO0", [128, 512], F32)
    psO1 = nc.alloc_psum_tensor("psO1", [128, 512], F32)
    psO = [psO0, psO1]

    p = Prog(nc)
    t_wo = p.dma('pool', wosb[:, :].rearrange("p (m f) -> p m f", m=8),
                 wos[:, :, :].rearrange("m p f -> p m f"), key='wo')
    t_idn = p.dma('pool', identb[:, :], identd[:, :], key='ident')
    t_um = p.dma('pool', umaskb[:, :], umaskd[:, :], key='umask')
    t_rm = p.dma('sp', rmb[:, :], rmaskd[:, :], key='rmask')
    p.wait('pe', (t_wo, t_idn, t_um))

    psfree = {}
    war = {}
    csl_sl = Slots(2)
    in_sl = Slots(2)
    xb_sl = Slots(2)
    ob_sl = Slots(2)

    p.serial(SERIAL_SECT >= 3 and SERIAL_SECT != 4)
    ts_tok = None
    for b in range(16):
        for r in range(NCORES):
            j, warc = csl_sl.acquire()
            tc_ = p.dma('sp', csl[:, j * 520:(j + 1) * 520],
                        callc[r * 2048 + b * 128:r * 2048 + (b + 1) * 128, :],
                        deps=warc, key=('csl', j))
            sview = sf[:, b * 520:(b + 1) * 520]
            if r == 0:
                ts_tok = p.op('dve', lambda v, j=j, sview=sview: v.tensor_scalar(
                    sview, csl[:, j * 520:(j + 1) * 520], rmb[:, 0:1], None,
                    ALU.mult), deps=(tc_, t_rm))
            else:
                ts_tok = p.op('dve', lambda v, j=j, r=r, sview=sview: v.scalar_tensor_tensor(
                    sview, csl[:, j * 520:(j + 1) * 520], rmb[:, r:r + 1], sview,
                    ALU.mult, ALU.add), deps=(tc_,))
            csl_sl.consumed(j, ts_tok)

    if SERIAL_SECT >= 3:
        p.serial(True)
    for b in range(16):
        if SERIAL_SECT == 5:
            p.serial(False)
        jj, warj = in_sl.acquire()
        o8 = jj * 8 * nt
        oc = jj * ntc * 1024
        ov = jj * ntc * 1040
        tq = p.dma('sp', qsb[:, o8:o8 + 8 * nt].rearrange("p (m t) -> p m t", m=8),
                   qfd[b].rearrange("(m p) t -> p m t", m=8), deps=warj, key=('ql', jj))
        tkf = p.dma('sp', ksb[:, o8:o8 + 8 * nt].rearrange("p (m t) -> p m t", m=8),
                    kfd[b].rearrange("(m p) t -> p m t", m=8), deps=warj, key=('kl', jj))
        tkt = p.dma('sp', ktsb[:, oc:oc + ntc * 1024].rearrange("p (c f) -> p c f", c=ntc),
                    ktmd[b].rearrange("(c p) f -> p c f", c=ntc), deps=warj, key=('ktl', jj))
        tvx = p.dma('sp', vxsb[:, ov:ov + ntc * 1040].rearrange("p (c f) -> p c f", c=ntc),
                    vexd[b].rearrange("(c p) f -> p c f", c=ntc), deps=warj, key=('vxl', jj))
        tld = (tq, tkf, tkt, tvx)

        tsb0 = p.op('act', lambda s, b=b: s.activation(
            sbf[:, :], sf[:, b * 520:(b + 1) * 520], AF.Identity),
            deps=(ts_tok, war.get('sbf')))
        sbf_tok = {h: tsb0 for h in range(16)}
        last_n2 = None
        ty_last = None
        yfm_evac = None
        for tc in range(ntc):
            p.serial(SERIAL_SECT in (1,) or (SERIAL_SECT >= 3 and SERIAL_SECT != 5))
            for h in range(16):
                q_ = h // 2
                rb, cb = (h % 2) * 64, (h // 2) * 65
                kview = ksb[rb:rb + 64, o8 + q_ * nt + tc * 128:o8 + q_ * nt + (tc + 1) * 128]
                qview = qsb[rb:rb + 64, o8 + q_ * nt + tc * 128:o8 + q_ * nt + (tc + 1) * 128]
                vview = vxsb[:, ov + tc * 1040 + h * 65:ov + tc * 1040 + (h + 1) * 65]
                ktview = ktsb[:, oc + tc * 1024 + h * 64:oc + tc * 1024 + (h + 1) * 64]
                sfview = sf[:, b * 520:(b + 1) * 520][rb:rb + 64, cb:cb + 65]
                sbview = sbf[rb:rb + 64, cb:cb + 65]
                tAt = p.op('pe', lambda t, kview=kview, qview=qview: t.matmul(
                    psAt[:, :], kview, qview, start=True, stop=True),
                    deps=(psfree.get('At'),) + (tld if (tc == 0 and h == 0) else ()))
                tAmt = p.op('dve', lambda v: v.tensor_mul(amt[:, :], psAt[:, :], umaskb[:, :]),
                            deps=(tAt, war.get('amt')))
                psfree['At'] = tAmt
                war['amt'] = None
                tn1 = p.op('pe', lambda t, vview=vview: t.matmul(
                    psN[:, 0:65], amt[:, :], vview, start=True, stop=False),
                    deps=(tAmt, psfree.get('N')), tick=False)
                tn2 = p.op('pe', lambda t, qview=qview, sbview=sbview: t.matmul(
                    psN[:, 0:65], qview, sbview, start=False, stop=True),
                    deps=(sbf_tok[h],))
                war['amt'] = tn2
                last_n2 = tn2
                tden = p.op('dve', lambda v: v.tensor_scalar_add(
                    denb[:, :], psN[:, 64:65], 1e-6), deps=(tn2, war.get('den')))
                trec = p.op('dve', lambda v: v.reciprocal(denb[:, :], denb[:, :]),
                            deps=(tden,))
                ty = p.op('dve', lambda v, h=h: v.tensor_scalar(
                    ytm[:, h * 64:(h + 1) * 64], psN[:, 0:64], denb[:, 0:1], None,
                    ALU.mult), deps=(trec,) + ((war.get('ytm'),) if h == 0 else ()))
                war['den'] = ty
                psfree['N'] = ty
                ty_last = ty
                if tc < ntc - 1:
                    tU = p.op('pe', lambda t, ktview=ktview, vview=vview: t.matmul(
                        psU[0:64, 0:65], ktview, vview, start=True, stop=True),
                        deps=(psfree.get('U'),))
                    tSa = p.op('dve', lambda v, sfview=sfview: v.tensor_add(
                        sfview, sfview, psU[0:64, 0:65]), deps=(tU, war.get('sfa')))
                    war['sfa'] = tSa
                    psfree['U'] = tSa
                    tSb = p.op('act', lambda s, sfview=sfview, sbview=sbview: s.activation(
                        sbview, sfview, AF.Identity), deps=(tSa, tn2))
                    sbf_tok[h] = tSb
            p.serial(SERIAL_SECT in (2,) or (SERIAL_SECT >= 3 and SERIAL_SECT != 5))
            ttr_l = None
            for m in range(8):
                ttr = p.op('pe', lambda t, m=m: t.transpose(
                    PT[:, :], ytm[:, m * 128:(m + 1) * 128], identb[:, :]),
                    deps=(ty_last, psfree.get('PT')))
                yfm_evac = p.op('act', lambda s, m=m, tc=tc: s.activation(
                    yfm[:, m * nt + tc * 128:m * nt + (tc + 1) * 128], PT[:, :],
                    AF.Identity),
                    deps=(ttr,) + ((war.get('yfm'),) if (tc == 0 and m == 0) else ()))
                psfree['PT'] = yfm_evac
                ttr_l = ttr
            war['ytm'] = ttr_l
        war['sbf'] = last_n2
        if SERIAL_SECT == 3:
            p.serial(False)
        elif SERIAL_SECT >= 3:
            p.serial(True)

        wo_last = None
        for m in range(8):
            P = psO[m % 2]
            pk = ('O', m % 2)
            tk = None
            for k in range(8):
                tk = p.op('pe', lambda t, m=m, k=k, P=P: t.matmul(
                    P[:, 0:nt], wosb[:, m * 1024 + k * 128:m * 1024 + (k + 1) * 128],
                    yfm[:, k * nt:(k + 1) * nt], start=(k == 0), stop=(k == 7)),
                    deps=(yfm_evac, psfree.get(pk)) if k == 0 else (),
                    tick=(k == 7))
            wo_last = tk
            tev = p.op('act', lambda s, P=P: s.activation(t1b[:, 0:nt], P[:, 0:nt], AF.Identity),
                       deps=(tk, war.get('t1b')))
            psfree[pk] = tev
            jx, warx = xb_sl.acquire()
            tx = p.dma('sp', xbuf[:, jx * 512:jx * 512 + nt],
                       xd[m * 128:(m + 1) * 128, b * nt:(b + 1) * nt], deps=warx,
                       key=('xb', jx))
            jo, waro = ob_sl.acquire()
            tadd = p.op('dve', lambda v, jx=jx, jo=jo: v.tensor_add(
                outb[:, jo * 512:jo * 512 + nt], t1b[:, 0:nt],
                xbuf[:, jx * 512:jx * 512 + nt]), deps=(tev, tx) + waro)
            war['t1b'] = tadd
            xb_sl.consumed(jx, tadd)
            tst = p.dma('sp', yd[m * 128:(m + 1) * 128, b * nt:(b + 1) * nt],
                        outb[:, jo * 512:jo * 512 + nt], deps=(tadd,), key=('yst', jo))
            ob_sl.consumed(jo, tst)
        war['yfm'] = wo_last
        in_sl.consumed(jj, last_n2)
        in_sl.consumed(jj, wo_last)

    p.final_dma_waits()
    p.emit()
    return nc


def l0attn_b_host_inputs(Wo, core_id):
    rmask = np.zeros((128, 8), np.float32)
    rmask[:, :core_id] = 1.0
    umask = np.triu(np.ones((128, 128), np.float32))
    return {"wos": prep_slabs(Wo), "identd": np.eye(128, dtype=np.float32),
            "umaskd": umask, "rmask": rmask}


def kernel(
    L0_attn_x, L0_attn_Wq, L0_attn_Wk, L0_attn_Wv, L0_attn_Wo,
    L0_ffn_w1_w, L0_ffn_w1_s, L0_ffn_w2_w, L0_ffn_w2_s,
    L0_ffn_h1_w, L0_ffn_h1_s, L0_ffn_h2_w, L0_ffn_h2_s,
    L0_ffn_w3_w, L0_ffn_w3_s,
    L1_attn_Wq, L1_attn_bq, L1_attn_Wk, L1_attn_bk, L1_attn_Wv, L1_attn_bv,
    L1_ffn_w1, L1_ffn_b1, L1_ffn_w2, L1_ffn_b2, L1_ffn_wr, L1_ffn_br,
):
    f32 = np.float32
    x = np.asarray(L0_attn_x, f32)
    cores = list(range(NCORES))
    xcs = [np.ascontiguousarray(
        x[:, TSLICE * c:TSLICE * (c + 1), :].reshape(TPC, D).T) for c in cores]

    ncA = build_l0attn_a()
    insA = l0attn_a_host_inputs(np.asarray(L0_attn_Wq, f32),
                                np.asarray(L0_attn_Wk, f32),
                                np.asarray(L0_attn_Wv, f32))
    resA = run_bass_kernel_spmd(ncA, [dict(insA, x=xcs[c]) for c in cores],
                                core_ids=cores)
    callc = np.concatenate([resA.results[c]["cst"] for c in cores], axis=0)
    ncB = build_l0attn_b()
    in_maps = []
    for c in cores:
        m = l0attn_b_host_inputs(np.asarray(L0_attn_Wo, f32), c)
        m.update(x=xcs[c], qf=resA.results[c]["qf"], kf=resA.results[c]["kf"],
                 ktm=resA.results[c]["ktm"], vex=resA.results[c]["vex"],
                 callc=callc)
        in_maps.append(m)
    resB = run_bass_kernel_spmd(ncB, in_maps, core_ids=cores)
    x1 = [resB.results[c]["y"] for c in cores]

    nc2 = build_l0ffn()
    ins2 = l0ffn_host_inputs(
        np.asarray(L0_ffn_w1_w, f32), np.asarray(L0_ffn_w1_s, f32),
        np.asarray(L0_ffn_w2_w, f32), np.asarray(L0_ffn_w2_s, f32),
        np.asarray(L0_ffn_h1_w, f32), np.asarray(L0_ffn_h1_s, f32),
        np.asarray(L0_ffn_h2_w, f32), np.asarray(L0_ffn_h2_s, f32),
        np.asarray(L0_ffn_w3_w, f32), np.asarray(L0_ffn_w3_s, f32))
    res2 = run_bass_kernel_spmd(nc2, [dict(ins2, x=x1[c]) for c in cores],
                                core_ids=cores)
    x2 = [res2.results[c]["y"] for c in cores]

    nc3 = build_l1attn()
    ins3 = l1attn_host_inputs(
        np.asarray(L1_attn_Wq, f32), np.asarray(L1_attn_bq, f32),
        np.asarray(L1_attn_Wk, f32), np.asarray(L1_attn_bk, f32),
        np.asarray(L1_attn_Wv, f32), np.asarray(L1_attn_bv, f32))
    res3 = run_bass_kernel_spmd(nc3, [dict(ins3, x=x2[c]) for c in cores],
                                core_ids=cores)
    x3 = [res3.results[c]["y"] for c in cores]

    nc4 = build_l1ffn()
    ins4 = l1ffn_host_inputs(
        np.asarray(L1_ffn_w1, f32), np.asarray(L1_ffn_b1, f32),
        np.asarray(L1_ffn_w2, f32), np.asarray(L1_ffn_b2, f32),
        np.asarray(L1_ffn_wr, f32), np.asarray(L1_ffn_br, f32))
    res4 = run_bass_kernel_spmd(nc4, [dict(ins4, x=x3[c]) for c in cores],
                                core_ids=cores)

    y = np.empty((BATCH, T, D), f32)
    for c in cores:
        y[:, TSLICE * c:TSLICE * (c + 1), :] = \
            res4.results[c]["y"].T.reshape(BATCH, TSLICE, D)
    return y

